# version 05 (final)
# baseline (speedup 1.0000x reference)
import numpy as np
import contextlib
import concourse.bass as bass
import concourse.mybir as mybir
from concourse.bass_utils import run_bass_kernel_spmd
from concourse.aot_env import allow_ici_all_to_all

F32 = mybir.dt.float32
BF16 = mybir.dt.bfloat16
I32 = mybir.dt.int32
AF = mybir.ActivationFunctionType
ALU = mybir.AluOpType

B = 2
S = 8192
D = 1024
NST = 16
RR = 64
HID = 3072
P = 128
TQ = 2048
TW = 1024
NC = 8
EPS = 1e-5
LN127 = float(np.log(127.0))
NCH = 8
CT = 256


class Sem:
    def __init__(self, h):
        self.h = h
        self.n = 0

    def dma(self):
        self.n += 16
        return self.n

    def inc(self):
        self.n += 1
        return self.n


def build_program():
    nc = bass.Bass("TRN2", target_bir_lowering=False, detect_race_conditions=False)

    xa = nc.dram_tensor("xa", [D, TQ], F32, kind="ExternalInput").ap()
    xb = nc.dram_tensor("xb", [B, P, S], F32, kind="ExternalInput").ap()
    xc = nc.dram_tensor("xc", [B, D, TW], F32, kind="ExternalInput").ap()
    wcat = nc.dram_tensor("wcat", [D, 96], F32, kind="ExternalInput").ap()
    wdt = nc.dram_tensor("wdt", [RR, P], F32, kind="ExternalInput").ap()
    bdt = nc.dram_tensor("bdt", [P, 1], F32, kind="ExternalInput").ap()
    alog = nc.dram_tensor("alog", [P, NST], F32, kind="ExternalInput").ap()
    dpv = nc.dram_tensor("dpv", [P, 1], F32, kind="ExternalInput").ap()
    lng = nc.dram_tensor("lng", [P, 8], F32, kind="ExternalInput").ap()
    lnb = nc.dram_tensor("lnb", [P, 8], F32, kind="ExternalInput").ap()
    wout = nc.dram_tensor("wout", [D, D], F32, kind="ExternalInput").ap()
    wg = nc.dram_tensor("wg", [D, HID], I32, kind="ExternalInput").ap()
    wu = nc.dram_tensor("wu", [D, HID], I32, kind="ExternalInput").ap()
    wd = nc.dram_tensor("wd", [HID, D], I32, kind="ExternalInput").ap()
    outc = nc.dram_tensor("outc", [B, D, TW], F32, kind="ExternalOutput").ap()

    ex1i = nc.dram_tensor("ex1i", [P, TQ], BF16).ap()
    ex1o = nc.dram_tensor("ex1o", [NC, P, TQ], BF16).ap()
    ex2i = nc.dram_tensor("ex2i", [NC, P, B, TW], BF16).ap()
    ex2o = nc.dram_tensor("ex2o", [NC, P, B, TW], BF16).ap()

    stP = contextlib.ExitStack()
    stA = contextlib.ExitStack()
    stB0 = contextlib.ExitStack()
    stB1 = contextlib.ExitStack()
    stB2 = contextlib.ExitStack()
    stC = contextlib.ExitStack()

    def sbuf(stk, name, shape, dt):
        return stk.enter_context(nc.sbuf_tensor(name, shape, dt))

    def psum(stk, name, shape):
        return stk.enter_context(nc.psum_tensor(name, shape, F32))

    cst1 = sbuf(stP, "cst1", [P, P], BF16)
    wcat_sb = sbuf(stP, "wcat_sb", [P, 8 * 96], BF16)
    wdt_sb = sbuf(stP, "wdt_sb", [RR, P], BF16)
    bdt_sb = sbuf(stP, "bdt_sb", [P, 1], F32)
    aneg_sb = sbuf(stP, "aneg_sb", [P, NST], F32)
    atmp_sb = sbuf(stP, "atmp_sb", [P, NST], F32)
    dp_sb = sbuf(stP, "dp_sb", [P, 1], F32)
    lng_sb = sbuf(stP, "lng_sb", [P, 8], F32)
    lnb_sb = sbuf(stP, "lnb_sb", [P, 8], F32)
    kc_sb = sbuf(stP, "kc_sb", [P, 3], F32)

    sem_names = ("sPRM", "sDT", "sMR", "sGL", "sRPa", "sRPb", "sEX", "sCC",
                 "sST", "sYC", "sXC", "sPE", "sDV", "sSC", "sWT", "sWO")
    SM = {k: Sem(stP.enter_context(nc.semaphore(k))) for k in sem_names}
    sPRM, sDT, sMR, sGL, sRPa, sRPb, sEX, sCC, sST, sYC, sXC, sPE, sDV, sSC, sWT, sWO = (
        SM[k] for k in sem_names)

    prog = {k: [] for k in ("sync", "scalar", "vector", "gpsimd", "tensor")}

    def emit(eng, fn):
        prog[eng].append(fn)

    def wait(eng, sem, val):
        if val > 0:
            emit(eng, lambda q, s=sem.h, v=val: q.wait_ge(s, v))

    xa_sb = sbuf(stA, "xa_sb", [P, 8 * TQ], BF16)
    x2_sb = sbuf(stA, "x2_sb", [P, 8 * TQ], BF16)
    ex1_src = sbuf(stA, "ex1_src", [P, TQ], BF16)
    mur_a = sbuf(stA, "mur_a", [P, TQ], F32)
    e2r_a = sbuf(stA, "e2r_a", [P, TQ], F32)
    var_a = sbuf(stA, "var_a", [P, TQ], F32)
    rstd_a = sbuf(stA, "rstd_a", [P, TQ], F32)
    mu_b = sbuf(stA, "mu_b", [P, TQ], BF16)
    rstd_b = sbuf(stA, "rstd_b", [P, TQ], BF16)

    emit("vector", lambda q: q.memset(cst1[:], 1.0 / 1024.0).then_inc(sDV.h, 1))
    sDV.inc()
    emit("vector", lambda q: q.memset(ex1_src[96:128, :], 0.0).then_inc(sDV.h, 1))
    sDV.inc()
    emit("vector", lambda q: q.memset(kc_sb[:, 0:1], EPS).then_inc(sDV.h, 1))
    sDV.inc()
    emit("vector", lambda q: q.memset(kc_sb[:, 1:2], -0.5).then_inc(sDV.h, 1))
    sDV.inc()
    emit("vector", lambda q: q.memset(kc_sb[:, 2:3], -LN127).then_inc(sDV.h, 1))
    cINIT = sDV.inc()

    emit("gpsimd", lambda q: q.dma_start(
        out=xa_sb[:].rearrange("p (dh t) -> p dh t", dh=8),
        in_=xa.rearrange("(dh p) t -> p dh t", p=P)).then_inc(sGL.h, 16))
    sGL.dma()
    emit("gpsimd", lambda q: q.dma_start(
        out=wcat_sb[:].rearrange("p (dh r) -> p dh r", dh=8),
        in_=wcat.rearrange("(dh p) r -> p dh r", p=P)).then_inc(sGL.h, 16))
    sGL.dma()
    emit("gpsimd", lambda q: q.dma_start(out=wdt_sb[:], in_=wdt).then_inc(sGL.h, 16))
    gA = sGL.dma()

    for t_sb, t_in in ((bdt_sb, bdt), (atmp_sb, alog), (dp_sb, dpv),
                       (lng_sb, lng), (lnb_sb, lnb)):
        emit("sync", lambda q, o=t_sb, i=t_in: q.dma_start(out=o[:], in_=i).then_inc(sPRM.h, 16))
        sPRM.dma()
    cPRM = sPRM.n

    wait("scalar", sPRM, cPRM)
    emit("scalar", lambda q: q.activation(atmp_sb[:], atmp_sb[:], AF.Exp).then_inc(sSC.h, 1))
    cAEXP = sSC.inc()
    wait("scalar", sGL, gA)
    emit("scalar", lambda q: q.activation(x2_sb[:], xa_sb[:], AF.Square).then_inc(sSC.h, 1))
    cSQ = sSC.inc()
    wait("vector", sSC, cAEXP)
    emit("vector", lambda q: q.tensor_scalar_mul(aneg_sb[:], atmp_sb[:], -1.0).then_inc(sDV.h, 1))
    cANEG = sDV.inc()

    stPS1 = contextlib.ExitStack()
    pmu = psum(stPS1, "pmu", [P, TQ])
    pe2 = psum(stPS1, "pe2", [P, TQ])
    wait("tensor", sGL, gA)
    wait("tensor", sDV, cINIT)
    for dh in range(8):
        for tc in range(4):
            def mm_mu(q, dh=dh, tc=tc):
                ins = q.matmul(pmu[:, tc * 512:(tc + 1) * 512], cst1[:],
                               xa_sb[:, dh * TQ + tc * 512: dh * TQ + (tc + 1) * 512],
                               start=(dh == 0), stop=(dh == 7))
                if dh == 7 and tc == 3:
                    ins.then_inc(sPE.h, 1)
            emit("tensor", mm_mu)
    cMU = sPE.inc()
    wait("tensor", sSC, cSQ)
    for dh in range(8):
        for tc in range(4):
            def mm_e2(q, dh=dh, tc=tc):
                ins = q.matmul(pe2[:, tc * 512:(tc + 1) * 512], cst1[:],
                               x2_sb[:, dh * TQ + tc * 512: dh * TQ + (tc + 1) * 512],
                               start=(dh == 0), stop=(dh == 7))
                if dh == 7 and tc == 3:
                    ins.then_inc(sPE.h, 1)
            emit("tensor", mm_e2)
    cE2 = sPE.inc()

    wait("vector", sPE, cMU)
    emit("vector", lambda q: q.tensor_copy(mur_a[:], pmu[:]).then_inc(sDV.h, 1))
    sDV.inc()
    wait("vector", sPE, cE2)
    emit("vector", lambda q: q.tensor_copy(e2r_a[:], pe2[:]).then_inc(sDV.h, 1))
    sDV.inc()
    emit("vector", lambda q: q.tensor_mul(var_a[:], mur_a[:], mur_a[:]).then_inc(sDV.h, 1))
    sDV.inc()
    emit("vector", lambda q: q.tensor_sub(var_a[:], e2r_a[:], var_a[:]).then_inc(sDV.h, 1))
    cVAR = sDV.inc()
    wait("scalar", sDV, cVAR)
    emit("scalar", lambda q: q.activation(rstd_a[:], var_a[:], AF.Ln, bias=kc_sb[:, 0:1]).then_inc(sSC.h, 1))
    sSC.inc()
    emit("scalar", lambda q: q.activation(rstd_a[:], rstd_a[:], AF.Exp, scale=kc_sb[:, 1:2]).then_inc(sSC.h, 1))
    cRSTD = sSC.inc()
    wait("vector", sSC, cRSTD)
    emit("vector", lambda q: q.tensor_copy(mu_b[:], mur_a[:]).then_inc(sDV.h, 1))
    sDV.inc()
    emit("vector", lambda q: q.tensor_copy(rstd_b[:], rstd_a[:]).then_inc(sDV.h, 1))
    sDV.inc()
    mu_bc = bass.AP(mu_b.ap().tensor, 0, [[TQ, P], [0, 8], [1, TQ]])
    rstd_bc = bass.AP(rstd_b.ap().tensor, 0, [[TQ, P], [0, 8], [1, TQ]])
    xa_v = xa_sb[:].rearrange("p (dh t) -> p dh t", dh=8)
    x2_v = x2_sb[:].rearrange("p (dh t) -> p dh t", dh=8)
    emit("vector", lambda q: q.tensor_tensor(x2_v, xa_v, mu_bc, ALU.subtract).then_inc(sDV.h, 1))
    sDV.inc()
    emit("vector", lambda q: q.tensor_tensor(xa_v, x2_v, rstd_bc, ALU.mult).then_inc(sDV.h, 1))
    cLN0 = sDV.inc()

    stPS1.close()
    pcat = psum(stA, "pcat", [96, TQ])
    wait("tensor", sDV, cLN0)
    for dh in range(8):
        for tc in range(4):
            def mm_cat(q, dh=dh, tc=tc):
                ins = q.matmul(pcat[:, tc * 512:(tc + 1) * 512],
                               wcat_sb[:, dh * 96:(dh + 1) * 96],
                               xa_sb[:, dh * TQ + tc * 512: dh * TQ + (tc + 1) * 512],
                               start=(dh == 0), stop=(dh == 7))
                if dh == 7 and tc == 3:
                    ins.then_inc(sPE.h, 1)
            emit("tensor", mm_cat)
    cCAT = sPE.inc()
    wait("vector", sPE, cCAT)
    emit("vector", lambda q: q.tensor_copy(ex1_src[0:96, :], pcat[:]).then_inc(sDV.h, 1))
    cEX1 = sDV.inc()

    wait("sync", sDV, cEX1)
    emit("sync", lambda q: q.dma_start(out=ex1_src[96:97, :], in_=mu_b[0:1, :]).then_inc(sMR.h, 16))
    sMR.dma()
    emit("sync", lambda q: q.dma_start(out=ex1_src[97:98, :], in_=rstd_b[0:1, :]).then_inc(sMR.h, 16))
    cROW = sMR.dma()
    wait("sync", sMR, cROW)
    emit("sync", lambda q: q.dma_start(out=ex1i, in_=ex1_src[:]).then_inc(sEX.h, 16))
    cEX1D = sEX.dma()
    wait("gpsimd", sEX, cEX1D)
    emit("gpsimd", lambda q: q.collective_compute(
        "AllGather", ALU.bypass, replica_groups=[list(range(NC))],
        ins=[ex1i], outs=[ex1o]).then_inc(sCC.h, 1))
    cAG = sCC.inc()

    stA.close()
    HF = S // 2

    delta_sb = sbuf(stB0, "delta_sb", [P, B * S], BF16)
    w_sb = sbuf(stB0, "w_sb", [P, B * S], BF16)
    yacc_sb = sbuf(stB0, "yacc_sb", [P, B * S], BF16)
    zsp_sb = sbuf(stB0, "zsp_sb", [P, 512], F32)
    dtb_sb = sbuf(stB1, "dtb_sb", [RR, B * S], BF16)
    ln0b_sb = sbuf(stB1, "ln0b_sb", [P, S], BF16)
    xbh_sb = sbuf(stB1, "xbh_sb", [P, S], BF16)
    murB_sb = sbuf(stB1, "murB_sb", [P, S], BF16)
    rstdB_sb = sbuf(stB1, "rstdB_sb", [P, S], BF16)

    wait("sync", sCC, cAG)
    emit("sync", lambda q: q.dma_start(
        out=dtb_sb[:].rearrange("r (b t) -> r b t", b=B),
        in_=bass.AP(ex1o.tensor, 0, [[TQ, RR], [4 * P * TQ, B], [P * TQ, 4], [1, TQ]]),
    ).then_inc(sDT.h, 16))
    cDTB = sDT.dma()

    pz = [psum(stB1, "pz0", [P, 512]), psum(stB1, "pz1", [P, 512])]
    wait("tensor", sDT, cDTB)
    cSP = []
    for i in range(B * 16):
        b, tc = divmod(i, 16)
        if i >= 2:
            wait("tensor", sSC, cSP[i - 2])
        def mm_z(q, b=b, tc=tc, i=i):
            q.matmul(pz[i % 2][:], wdt_sb[:],
                     dtb_sb[:, b * S + tc * 512: b * S + (tc + 1) * 512],
                     start=True, stop=True).then_inc(sPE.h, 1)
        emit("tensor", mm_z)
        cz = sPE.inc()
        wait("scalar", sPE, cz)
        def sp_z(q, b=b, tc=tc, i=i):
            q.activation(zsp_sb[:], pz[i % 2][:], AF.Exp, bias=bdt_sb[:, 0:1])
            q.activation(delta_sb[:, b * S + tc * 512: b * S + (tc + 1) * 512],
                         zsp_sb[:], AF.Ln, bias=1.0).then_inc(sSC.h, 1)
        emit("scalar", sp_z)
        sSC.inc()
        cSP.append(sSC.n)
    cSPall = sSC.n

    cSUB = {}
    cMULT = {}
    cW0 = {}
    for b in range(B):
        if b == 0:
            wait("gpsimd", sCC, cAG)
        else:
            wait("gpsimd", sDV, cSUB[0])
        emit("gpsimd", lambda q, b=b: q.dma_start(out=xbh_sb[:], in_=xb[b]).then_inc(sGL.h, 16))
        gXB = sGL.dma()
        if b == 0:
            wait("sync", sCC, cAG)
        else:
            wait("sync", sDV, cMULT[0])
        def rep_mu(q, b=b):
            q.dma_start(out=murB_sb[:],
                        in_=bass.AP(ex1o.tensor, (b * 4) * P * TQ + 96 * TQ,
                                    [[0, P], [P * TQ, 4], [1, TQ]])).then_inc(sMR.h, 16)
            q.dma_start(out=rstdB_sb[:],
                        in_=bass.AP(ex1o.tensor, (b * 4) * P * TQ + 97 * TQ,
                                    [[0, P], [P * TQ, 4], [1, TQ]])).then_inc(sMR.h, 16)
        emit("sync", rep_mu)
        sMR.dma()
        lMR = sMR.dma()
        wait("vector", sGL, gXB)
        wait("vector", sMR, lMR)
        emit("vector", lambda q: q.tensor_sub(ln0b_sb[:], xbh_sb[:], murB_sb[:]).then_inc(sDV.h, 1))
        cSUB[b] = sDV.inc()
        emit("vector", lambda q: q.tensor_mul(ln0b_sb[:], ln0b_sb[:], rstdB_sb[:]).then_inc(sDV.h, 1))
        cMULT[b] = sDV.inc()
        if b == 0:
            wait("vector", sSC, cSPall)
        emit("vector", lambda q, b=b: q.tensor_mul(
            w_sb[:, b * S:(b + 1) * S], delta_sb[:, b * S:(b + 1) * S], ln0b_sb[:]).then_inc(sDV.h, 1))
        sDV.inc()
        emit("vector", lambda q, b=b: q.tensor_scalar(
            yacc_sb[:, b * S:(b + 1) * S], ln0b_sb[:], dp_sb[:, 0:1], None,
            ALU.mult).then_inc(sDV.h, 1))
        cW0[b] = sDV.inc()
    cPREP = cW0[B - 1]

    stB1.close()
    a_buf = sbuf(stB2, "a_buf", [P, HF], BF16)
    u_buf = sbuf(stB2, "u_buf", [P, HF], BF16)
    h_buf = sbuf(stB2, "h_buf", [P, S], BF16)
    bc_rep = [sbuf(stB2, "bc_rep0", [P, 2 * HF], BF16),
              sbuf(stB2, "bc_rep1", [P, 2 * HF], BF16)]

    NIT = B * NST * 2
    scan_done = {}
    ch_done = {}
    rep_sems = [sRPa, sRPb]

    def emit_rep(i, waitv):
        bn, half = divmod(i, 2)
        b, n = divmod(bn, NST)
        par = i % 2
        if waitv:
            wait("sync", sDV, waitv)
        def rep(q, b=b, n=n, half=half, par=par):
            base = (b * 4 + half * 2) * P * TQ
            q.dma_start(
                out=bc_rep[par][:, 0:HF],
                in_=bass.AP(ex1o.tensor, base + (64 + n) * TQ,
                            [[0, P], [P * TQ, 2], [1, TQ]]),
            ).then_inc(rep_sems[par].h, 16)
            q.dma_start(
                out=bc_rep[par][:, HF:2 * HF],
                in_=bass.AP(ex1o.tensor, base + (80 + n) * TQ,
                            [[0, P], [P * TQ, 2], [1, TQ]]),
            ).then_inc(rep_sems[par].h, 16)
        emit("sync", rep)
        rep_sems[par].dma()
        rep_sems[par].dma()

    emit_rep(0, cPREP)
    emit_rep(1, 0)

    for i in range(NIT):
        bn, half = divmod(i, 2)
        b, n = divmod(bn, NST)
        tlo = b * S + half * HF
        if i == 0:
            wait("scalar", sDV, cANEG)
            wait("scalar", sDV, cPREP)
        else:
            wait("scalar", sDV, scan_done[i - 1])
        def a_exp(q, tlo=tlo, n=n):
            q.activation(a_buf[:], delta_sb[:, tlo:tlo + HF], AF.Exp,
                         scale=aneg_sb[:, n:n + 1]).then_inc(sSC.h, 1)
        emit("scalar", a_exp)
        cA = sSC.inc()

        par = i % 2
        wait("vector", rep_sems[par], rep_sems[par].n)
        if i == 0:
            wait("vector", sDV, cPREP)
        emit("vector", lambda q, tlo=tlo, par=par: q.tensor_mul(
            u_buf[:], w_sb[:, tlo:tlo + HF], bc_rep[par][:, 0:HF]).then_inc(sDV.h, 1))
        sDV.inc()
        wait("vector", sSC, cA)
        if half == 0:
            emit("vector", lambda q: q.tensor_tensor_scan(
                h_buf[:, 0:HF], a_buf[:], u_buf[:], 0.0,
                ALU.mult, ALU.add).then_inc(sDV.h, 1))
        else:
            emit("vector", lambda q: q.tensor_tensor_scan(
                h_buf[:, HF:S], a_buf[:], u_buf[:], h_buf[:, HF - 1:HF],
                ALU.mult, ALU.add).then_inc(sDV.h, 1))
        scan_done[i] = sDV.inc()
        emit("vector", lambda q, half=half, par=par: q.tensor_mul(
            u_buf[:], h_buf[:, half * HF:(half + 1) * HF], bc_rep[par][:, HF:2 * HF]).then_inc(sDV.h, 1))
        sDV.inc()
        emit("vector", lambda q, tlo=tlo: q.tensor_add(
            yacc_sb[:, tlo:tlo + HF], yacc_sb[:, tlo:tlo + HF],
            u_buf[:]).then_inc(sDV.h, 1))
        ch_done[i] = sDV.inc()
        if i + 2 < NIT:
            emit_rep(i + 2, ch_done[i])

    cBdone = sDV.n

    wait("sync", sDV, cBdone)
    emit("sync", lambda q: q.dma_start(
        out=bass.AP(ex2i.tensor, 0, [[B * TW, P], [TW, B], [P * B * TW, NC], [1, TW]]),
        in_=yacc_sb[:].rearrange("p (b s w) -> p b s w", b=B, s=NC),
    ).then_inc(sEX.h, 16))
    cEX2D = sEX.dma()
    wait("gpsimd", sEX, cEX2D)
    def a2a(q):
        with allow_ici_all_to_all():
            q.collective_compute(
                "AllToAll", ALU.bypass, replica_groups=[list(range(NC))],
                ins=[ex2i], outs=[ex2o]).then_inc(sCC.h, 1)
    emit("gpsimd", a2a)
    cA2A = sCC.inc()

    stB2.close()
    stB0.close()

    wout_sb = sbuf(stC, "wout_sb", [P, 8 * D], BF16)
    wg_sb = sbuf(stC, "wg_sb", [P, 8 * HID], BF16)
    wu_sb = sbuf(stC, "wu_sb", [P, 8 * HID], BF16)
    wd_sb = sbuf(stC, "wd_sb", [P, 24 * D], BF16)
    y_ch = sbuf(stC, "y_ch", [P, 8 * CT], BF16)
    y2_ch = sbuf(stC, "y2_ch", [P, 8 * CT], BF16)
    x_ch = sbuf(stC, "x_ch", [P, 8 * CT], BF16)
    x1_ch = sbuf(stC, "x1_ch", [P, 8 * CT], BF16)
    ln1_ch = sbuf(stC, "ln1_ch", [P, 8 * CT], BF16)
    out_sb = sbuf(stC, "out_sb", [P, 8 * CT], F32)
    muf_c = sbuf(stC, "muf_c", [P, CT], F32)
    varf_c = sbuf(stC, "varf_c", [P, CT], F32)
    mub_c = sbuf(stC, "mub_c", [P, CT], BF16)
    rstdb_c = sbuf(stC, "rstdb_c", [P, CT], BF16)
    sg_sb = [sbuf(stC, "sg0", [P, CT], BF16), sbuf(stC, "sg1", [P, CT], BF16)]
    h_all = sbuf(stC, "h_all", [P, 24 * CT], BF16)

    pstat = psum(stC, "pstat", [P, 512])
    pmm = psum(stC, "pmm", [P, 8 * CT])
    pgu = [psum(stC, "pgu0", [P, 512]), psum(stC, "pgu1", [P, 512])]

    wait("gpsimd", sEX, cEX2D)
    emit("gpsimd", lambda q: q.dma_start(
        out=wout_sb[:].rearrange("p (dh e) -> p dh e", dh=8),
        in_=wout.rearrange("(dh p) e -> p dh e", p=P)).then_inc(sWO.h, 16))
    gWOUT = sWO.dma()
    for dh in range(8):
        emit("gpsimd", lambda q, dh=dh: q.dma_start(
            out=wg_sb[:, dh * HID:(dh + 1) * HID],
            in_=wg.rearrange("(dh p) h -> p dh h", p=P)[:, dh, :]).then_inc(sWT.h, 16))
        sWT.dma()
    gWG = sWT.n
    for dh in range(8):
        emit("gpsimd", lambda q, dh=dh: q.dma_start(
            out=wu_sb[:, dh * HID:(dh + 1) * HID],
            in_=wu.rearrange("(dh p) h -> p dh h", p=P)[:, dh, :]).then_inc(sWT.h, 16))
        sWT.dma()
    gWU = sWT.n
    for hh in range(24):
        emit("gpsimd", lambda q, hh=hh: q.dma_start(
            out=wd_sb[:, hh * D:(hh + 1) * D],
            in_=wd.rearrange("(hh p) d -> p hh d", p=P)[:, hh, :]).then_inc(sWT.h, 16))
        sWT.dma()
    gWD = sWT.n

    prev = {"yh": 0, "x1add": 0, "outstt": 0, "lnstats_evac": 0, "xadd": 0,
            "ych_sub": 0, "xch_read": 0, "y2free": 0}
    silu_cnt = {}
    hmul_cnt = {}
    y_chv = y_ch[:].rearrange("p (dh w) -> p dh w", dh=8)
    y2_chv = y2_ch[:].rearrange("p (dh w) -> p dh w", dh=8)
    x1_chv = x1_ch[:].rearrange("p (dh w) -> p dh w", dh=8)
    ln1_chv = ln1_ch[:].rearrange("p (dh w) -> p dh w", dh=8)
    mub_bc = bass.AP(mub_c.ap().tensor, 0, [[CT, P], [0, 8], [1, CT]])
    rstdb_bc = bass.AP(rstdb_c.ap().tensor, 0, [[CT, P], [0, 8], [1, CT]])

    for ci in range(NCH):
        b, tc = divmod(ci, 4)
        if ci == 0:
            wait("sync", sCC, cA2A)
        else:
            wait("sync", sDV, prev["ych_sub"])
        def dma_y(q, b=b, tc=tc):
            q.dma_start(out=y_ch[:],
                        in_=bass.AP(ex2o.tensor, b * TW + tc * CT,
                                    [[B * TW, P], [P * B * TW, 8], [1, CT]])
                        ).then_inc(sYC.h, 16)
        emit("sync", dma_y)
        cYC = sYC.dma()
        if ci > 0:
            wait("gpsimd", sDV, prev["xch_read"])
        def dma_x(q, b=b, tc=tc):
            q.dma_start(out=x_ch[:],
                        in_=bass.AP(xc.tensor, b * D * TW + tc * CT,
                                    [[TW, P], [P * TW, 8], [1, CT]])
                        ).then_inc(sXC.h, 16)
        emit("gpsimd", dma_x)
        cXC = sXC.dma()

        wait("scalar", sYC, cYC)
        if ci > 0:
            wait("scalar", sPE, prev["y2free"])
        emit("scalar", lambda q: q.activation(y2_ch[:], y_ch[:], AF.Square).then_inc(sSC.h, 1))
        cYSQ = sSC.inc()
        wait("tensor", sYC, cYC)
        if ci > 0:
            wait("tensor", sDV, prev["lnstats_evac"])
        for dh in range(8):
            def mm_smu(q, dh=dh):
                ins = q.matmul(pstat[:, 0:CT], cst1[:], y_chv[:, dh, :],
                               start=(dh == 0), stop=(dh == 7))
                if dh == 7:
                    ins.then_inc(sPE.h, 1)
            emit("tensor", mm_smu)
        cSMU = sPE.inc()
        wait("tensor", sSC, cYSQ)
        for dh in range(8):
            def mm_se2(q, dh=dh):
                ins = q.matmul(pstat[:, CT:2 * CT], cst1[:], y2_chv[:, dh, :],
                               start=(dh == 0), stop=(dh == 7))
                if dh == 7:
                    ins.then_inc(sPE.h, 1)
            emit("tensor", mm_se2)
        cSE2 = sPE.inc()
        wait("vector", sPE, cSE2)
        emit("vector", lambda q: q.tensor_copy(muf_c[:], pstat[:, 0:CT]).then_inc(sDV.h, 1))
        sDV.inc()
        emit("vector", lambda q: q.tensor_mul(varf_c[:], muf_c[:], muf_c[:]).then_inc(sDV.h, 1))
        sDV.inc()
        emit("vector", lambda q: q.tensor_sub(varf_c[:], pstat[:, CT:2 * CT], varf_c[:]).then_inc(sDV.h, 1))
        cVARC = sDV.inc()
        wait("scalar", sDV, cVARC)
        emit("scalar", lambda q: q.activation(varf_c[:], varf_c[:], AF.Ln, bias=kc_sb[:, 0:1]).then_inc(sSC.h, 1))
        sSC.inc()
        emit("scalar", lambda q: q.activation(varf_c[:], varf_c[:], AF.Exp, scale=kc_sb[:, 1:2]).then_inc(sSC.h, 1))
        cRSTDC = sSC.inc()
        wait("vector", sSC, cRSTDC)
        emit("vector", lambda q: q.tensor_copy(mub_c[:], muf_c[:]).then_inc(sDV.h, 1))
        sDV.inc()
        emit("vector", lambda q: q.tensor_copy(rstdb_c[:], varf_c[:]).then_inc(sDV.h, 1))
        cCAST = sDV.inc()
        emit("vector", lambda q: q.tensor_tensor(y2_chv, y_chv, mub_bc, ALU.subtract).then_inc(sDV.h, 1))
        prev["ych_sub"] = sDV.inc()
        emit("vector", lambda q: q.tensor_tensor(y2_chv, y2_chv, rstdb_bc, ALU.mult).then_inc(sDV.h, 1))
        sDV.inc()
        for dh in range(8):
            def ln_aff(q, dh=dh):
                ins = q.tensor_scalar(y2_chv[:, dh, :], y2_chv[:, dh, :],
                                      lng_sb[:, dh:dh + 1], lnb_sb[:, dh:dh + 1],
                                      ALU.mult, ALU.add)
                if dh == 7:
                    ins.then_inc(sDV.h, 1)
            emit("vector", ln_aff)
        cYH = sDV.inc()
        prev["yh"] = cYH

        wait("tensor", sDV, cYH)
        wait("tensor", sWO, gWOUT)
        if ci > 0:
            wait("tensor", sDV, prev["outstt"])
        for eh in range(8):
            for dh in range(8):
                def mm_att(q, eh=eh, dh=dh):
                    ins = q.matmul(pmm[:, eh * CT:(eh + 1) * CT],
                                   wout_sb[:, dh * D + eh * P: dh * D + (eh + 1) * P],
                                   y2_chv[:, dh, :],
                                   start=(dh == 0), stop=(dh == 7))
                    if eh == 7 and dh == 7:
                        ins.then_inc(sPE.h, 1)
                emit("tensor", mm_att)
        cATT = sPE.inc()

        wait("vector", sPE, cATT)
        wait("vector", sXC, cXC)
        emit("vector", lambda q: q.tensor_add(x1_ch[:], pmm[:], x_ch[:]).then_inc(sDV.h, 1))
        cX1 = sDV.inc()
        prev["xadd"] = cX1
        prev["xch_read"] = cX1
        cX1B = cX1

        wait("scalar", sDV, cX1B)
        wait("scalar", sPE, cATT)
        emit("scalar", lambda q: q.activation(y2_ch[:], x1_ch[:], AF.Square).then_inc(sSC.h, 1))
        cX1SQ = sSC.inc()
        wait("tensor", sDV, cCAST)
        wait("tensor", sDV, cX1B)
        for dh in range(8):
            def mm_lmu(q, dh=dh):
                ins = q.matmul(pstat[:, 0:CT], cst1[:], x1_chv[:, dh, :],
                               start=(dh == 0), stop=(dh == 7))
                if dh == 7:
                    ins.then_inc(sPE.h, 1)
            emit("tensor", mm_lmu)
        cLMU = sPE.inc()
        wait("tensor", sSC, cX1SQ)
        for dh in range(8):
            def mm_le2(q, dh=dh):
                ins = q.matmul(pstat[:, CT:2 * CT], cst1[:], y2_chv[:, dh, :],
                               start=(dh == 0), stop=(dh == 7))
                if dh == 7:
                    ins.then_inc(sPE.h, 1)
            emit("tensor", mm_le2)
        cLE2 = sPE.inc()
        prev["y2free"] = cLE2
        wait("vector", sPE, cLE2)
        emit("vector", lambda q: q.tensor_copy(muf_c[:], pstat[:, 0:CT]).then_inc(sDV.h, 1))
        sDV.inc()
        emit("vector", lambda q: q.tensor_mul(varf_c[:], muf_c[:], muf_c[:]).then_inc(sDV.h, 1))
        sDV.inc()
        emit("vector", lambda q: q.tensor_sub(varf_c[:], pstat[:, CT:2 * CT], varf_c[:]).then_inc(sDV.h, 1))
        cVAR1 = sDV.inc()
        wait("scalar", sDV, cVAR1)
        emit("scalar", lambda q: q.activation(varf_c[:], varf_c[:], AF.Ln, bias=kc_sb[:, 0:1]).then_inc(sSC.h, 1))
        sSC.inc()
        emit("scalar", lambda q: q.activation(varf_c[:], varf_c[:], AF.Exp,
                                              scale=kc_sb[:, 1:2], bias=kc_sb[:, 2:3]).then_inc(sSC.h, 1))
        cRSTD1 = sSC.inc()
        wait("vector", sSC, cRSTD1)
        emit("vector", lambda q: q.tensor_copy(mub_c[:], muf_c[:]).then_inc(sDV.h, 1))
        sDV.inc()
        emit("vector", lambda q: q.tensor_copy(rstdb_c[:], varf_c[:]).then_inc(sDV.h, 1))
        cCAST1 = sDV.inc()
        prev["lnstats_evac"] = cCAST1
        emit("vector", lambda q: q.tensor_tensor(ln1_chv, x1_chv, mub_bc, ALU.subtract).then_inc(sDV.h, 1))
        sDV.inc()
        emit("vector", lambda q: q.tensor_tensor(ln1_chv, ln1_chv, rstdb_bc, ALU.mult).then_inc(sDV.h, 1))
        cLN1 = sDV.inc()

        wait("tensor", sDV, cLN1)
        wait("tensor", sDV, cX1)
        if ci == 0:
            wait("tensor", sWT, gWD)
        for hh in range(24):
            par = hh % 2
            gidx = ci * 24 + hh
            if gidx >= 2:
                wait("tensor", sSC, silu_cnt[gidx - 2])
                wait("tensor", sDV, hmul_cnt[gidx - 2])
            for dh in range(8):
                def mm_g(q, hh=hh, dh=dh, par=par):
                    q.matmul(pgu[par][:, 0:CT],
                             wg_sb[:, dh * HID + hh * P: dh * HID + (hh + 1) * P],
                             ln1_chv[:, dh, :], start=(dh == 0), stop=(dh == 7))
                emit("tensor", mm_g)
            for dh in range(8):
                def mm_u(q, hh=hh, dh=dh, par=par):
                    ins = q.matmul(pgu[par][:, CT:2 * CT],
                                   wu_sb[:, dh * HID + hh * P: dh * HID + (hh + 1) * P],
                                   ln1_chv[:, dh, :], start=(dh == 0), stop=(dh == 7))
                    if dh == 7:
                        ins.then_inc(sPE.h, 1)
                emit("tensor", mm_u)
            cGU = sPE.inc()
            wait("scalar", sPE, cGU)
            def silu(q, par=par):
                q.activation(sg_sb[par][:], pgu[par][:, 0:CT], AF.Silu).then_inc(sSC.h, 1)
            emit("scalar", silu)
            silu_cnt[gidx] = sSC.inc()
            wait("vector", sSC, silu_cnt[gidx])
            def hmul(q, par=par, hh=hh):
                q.tensor_mul(h_all[:, hh * CT:(hh + 1) * CT], sg_sb[par][:],
                             pgu[par][:, CT:2 * CT]).then_inc(sDV.h, 1)
            emit("vector", hmul)
            hmul_cnt[gidx] = sDV.inc()
        wait("tensor", sDV, hmul_cnt[ci * 24 + 23])
        for dh in range(8):
            for hh in range(24):
                def mm_d(q, hh=hh, dh=dh):
                    ins = q.matmul(pmm[:, dh * CT:(dh + 1) * CT],
                                   wd_sb[:, hh * D + dh * P: hh * D + (dh + 1) * P],
                                   h_all[:, hh * CT:(hh + 1) * CT],
                                   start=(hh == 0), stop=(hh == 23))
                    if dh == 7 and hh == 23:
                        ins.then_inc(sPE.h, 1)
                emit("tensor", mm_d)
        cDOWN = sPE.inc()

        wait("vector", sPE, cDOWN)
        if ci > 0:
            wait("vector", sST, prev["outdma"])
        emit("vector", lambda q: q.scalar_tensor_tensor(
            out_sb[:], pmm[:], 1.0 / 127.0, x1_ch[:], ALU.mult, ALU.add).then_inc(sDV.h, 1))
        cOUT = sDV.inc()
        prev["outstt"] = cOUT
        wait("sync", sDV, cOUT)
        def dma_out(q, b=b, tc=tc):
            q.dma_start(out=bass.AP(outc.tensor, b * D * TW + tc * CT,
                                    [[TW, P], [P * TW, 8], [1, CT]]),
                        in_=out_sb[:]).then_inc(sST.h, 16)
        emit("sync", dma_out)
        prev["outdma"] = sST.dma()

    wait("sync", sST, sST.n)

    with nc.Block() as block:
        @block.sync
        def _(q):
            for fn in prog["sync"]:
                fn(q)

        @block.scalar
        def _(q):
            for fn in prog["scalar"]:
                fn(q)

        @block.vector
        def _(q):
            for fn in prog["vector"]:
                fn(q)

        @block.gpsimd
        def _(q):
            for fn in prog["gpsimd"]:
                fn(q)

        @block.tensor
        def _(q):
            for fn in prog["tensor"]:
                fn(q)

    stC.close()
    stP.close()
    return nc


_NC_CACHE = {}


def _get_program():
    if "nc" not in _NC_CACHE:
        _NC_CACHE["nc"] = build_program()
    return _NC_CACHE["nc"]


def kernel(L0_attn_x, L0_attn_W_dtw, L0_attn_W_dt, L0_attn_b_dt, L0_attn_A_log,
           L0_attn_W_B, L0_attn_W_C, L0_attn_Dp, L0_attn_ln_g, L0_attn_ln_b,
           L0_attn_W_out, L0_ffn_w_gate, L0_ffn_w_up, L0_ffn_w_down):
    x = np.asarray(L0_attn_x, dtype=np.float32)
    x_t = np.ascontiguousarray(x.transpose(0, 2, 1))
    wcat = np.ascontiguousarray(
        np.concatenate([np.asarray(L0_attn_W_dtw).T, np.asarray(L0_attn_W_B).T,
                        np.asarray(L0_attn_W_C).T], axis=1).astype(np.float32))
    wdt_t = np.ascontiguousarray(np.asarray(L0_attn_W_dt, dtype=np.float32).T)
    bdt = np.asarray(L0_attn_b_dt, dtype=np.float32)
    alog = np.asarray(L0_attn_A_log, dtype=np.float32)
    dp = np.asarray(L0_attn_Dp, dtype=np.float32)
    lng = np.ascontiguousarray(np.asarray(L0_attn_ln_g, dtype=np.float32).reshape(8, P).T)
    lnb = np.ascontiguousarray(np.asarray(L0_attn_ln_b, dtype=np.float32).reshape(8, P).T)
    wout_t = np.ascontiguousarray(np.asarray(L0_attn_W_out, dtype=np.float32).T)
    wg_t = np.ascontiguousarray(np.asarray(L0_ffn_w_gate, dtype=np.int32).T)
    wu_t = np.ascontiguousarray(np.asarray(L0_ffn_w_up, dtype=np.int32).T)
    wd_t = np.ascontiguousarray(np.asarray(L0_ffn_w_down, dtype=np.int32).T)

    in_maps = []
    for r in range(NC):
        b, q = divmod(r, 4)
        in_maps.append({
            "xa": np.ascontiguousarray(x_t[b][:, q * TQ:(q + 1) * TQ]),
            "xb": np.ascontiguousarray(x_t[:, r * P:(r + 1) * P, :]),
            "xc": np.ascontiguousarray(x_t[:, :, r * TW:(r + 1) * TW]),
            "wcat": wcat,
            "wdt": np.ascontiguousarray(wdt_t[:, r * P:(r + 1) * P]),
            "bdt": np.ascontiguousarray(bdt[r * P:(r + 1) * P].reshape(P, 1)),
            "alog": np.ascontiguousarray(alog[r * P:(r + 1) * P]),
            "dpv": np.ascontiguousarray(dp[r * P:(r + 1) * P].reshape(P, 1)),
            "lng": lng,
            "lnb": lnb,
            "wout": wout_t,
            "wg": wg_t,
            "wu": wu_t,
            "wd": wd_t,
        })

    nc = _get_program()
    res = run_bass_kernel_spmd(nc, in_maps, core_ids=list(range(NC)))

    out = np.empty((B, S, D), dtype=np.float32)
    for r in range(NC):
        oc = np.asarray(res.results[r]["outc"])
        for b in range(B):
            out[b, r * TW:(r + 1) * TW, :] = oc[b].T
    return out
